# version 01 (final)
# baseline (speedup 1.0000x reference)
import math
import numpy as np
import concourse.bass as bass
import concourse.mybir as mybir
from concourse.bass_utils import run_bass_kernel_spmd

F32 = mybir.dt.float32
U8 = mybir.dt.uint8
AF = mybir.ActivationFunctionType
ALU = mybir.AluOpType
AX = mybir.AxisListType

NCORE = 8
S = 2048
D = 1024


class Ctx:
    def __init__(self, nc):
        self.nc = nc
        self.ct = 0
        self.prog = {"sync": [], "tensor": [], "vector": [], "scalar": []}

    def emit(self, eng, fn, inc=1):
        ct = self.ct

        def run(handle, sem, ct=ct, fn=fn, inc=inc):
            handle.wait_ge(sem, ct)
            fn().then_inc(sem, inc)

        self.prog[eng].append(run)
        self.ct += inc

    def dma(self, dst, src):
        nc = self.nc
        self.emit("sync", lambda: nc.sync.dma_start(out=dst, in_=src), inc=16)

    def t(self, fn):
        self.emit("tensor", fn)

    def v(self, fn):
        self.emit("vector", fn)

    def s(self, fn):
        self.emit("scalar", fn)

    def finalize(self):
        nc = self.nc
        prog = self.prog
        total = self.ct
        with nc.semaphore() as sem, nc.Block() as blk:

            @blk.sync
            def _(sync):
                for r in prog["sync"]:
                    r(sync, sem)
                sync.wait_ge(sem, total)

            @blk.tensor
            def _(te):
                for r in prog["tensor"]:
                    r(te, sem)

            @blk.vector
            def _(ve):
                for r in prog["vector"]:
                    r(ve, sem)

            @blk.scalar
            def _(se):
                for r in prog["scalar"]:
                    r(se, sem)


def ceil_div(a, b):
    return (a + b - 1) // b


class Builder:

    def __init__(self, nc, ctx, es):
        self.nc, self.ctx = nc, ctx
        self.a_sb = es(nc.sbuf_tensor("a_sb", [128, 4096], F32))
        self.at_sb = es(nc.sbuf_tensor("at_sb", [128, 4096], F32))
        self.b_sb = es(nc.sbuf_tensor("b_sb", [128, 4096], F32))
        self.row = es(nc.sbuf_tensor("row_sb", [128, 4096], F32))
        self.sc1 = es(nc.sbuf_tensor("sc1", [128, 4096], F32))
        self.sc2 = es(nc.sbuf_tensor("sc2", [128, 1024], F32))
        self.col1 = es(nc.sbuf_tensor("col1", [128, 4], F32))
        self.col2 = es(nc.sbuf_tensor("col2", [128, 4], F32))
        self.col3 = es(nc.sbuf_tensor("col3", [128, 4], F32))
        self.ones = es(nc.sbuf_tensor("ones_sb", [1, 128], F32))
        self.tiny = es(nc.sbuf_tensor("tiny_sb", [1, 1536], F32))
        self.ident = es(nc.sbuf_tensor("ident_sb", [128, 128], F32))
        self.ksb = es(nc.sbuf_tensor("ksb", [1, 64], F32))
        self.kt = es(nc.sbuf_tensor("kt_sb", [128, 64], F32))
        self.kmap = {}
        self.ps = [es(nc.psum_tensor("ps%d" % i, [128, 512], F32)) for i in range(4)]
        self.pst = es(nc.psum_tensor("pst", [128, 128], F32))
        self.psb = es(nc.psum_tensor("psb", [128, 512], F32))

    def load_ident(self, ident_ap, ones_ap, kvec_ap):
        self.ctx.dma(self.ident[:, :], ident_ap[:, :])
        self.ctx.dma(self.ones[0:1, 0:128], ones_ap[0:1, 0:128])
        self.ctx.dma(self.kt[0:128, 0:64], kvec_ap[0:128, 0:64])
        for v in (0.0, 1.0, -1.0):
            self.konst(v)

    def konst(self, v):
        v = float(v)
        if v not in self.kmap:
            i = len(self.kmap)
            assert i < 64
            self.kmap[v] = i
            self.nc.const_aps.register(F32, v, self.kt[0:128, i:i + 1])
        return v

    def kcol(self, v):
        self.konst(v)
        return self.kt[0:128, self.kmap[float(v)]:self.kmap[float(v)] + 1]

    def bc_row(self, row_ap, n, dst):
        nc, ctx = self.nc, self.ctx
        ctx.dma(self.tiny[0:1, 0:n], row_ap[0:1, 0:n])
        for j in range(ceil_div(n, 512)):
            nn = min(512, n - j * 512)
            ctx.t(lambda j=j, nn=nn: nc.tensor.matmul(
                self.psb[0:128, 0:nn], self.ones[0:1, 0:128],
                self.tiny[0:1, j * 512:j * 512 + nn], start=True, stop=True))
            ctx.s(lambda j=j, nn=nn: nc.scalar.copy(
                dst[0:128, j * 512:j * 512 + nn], self.psb[0:128, 0:nn]))

    def transpose_to(self, src_sb, kk, dst_sb):
        nc, ctx = self.nc, self.ctx
        ctx.t(lambda: nc.tensor.transpose(self.pst[0:kk, 0:128], src_sb, self.ident[:, :]))
        ctx.s(lambda: nc.scalar.copy(dst_sb[0:kk, 0:128], self.pst[0:kk, 0:128]))

    def ln(self, t_in, t_out, ddim, eps=1e-5, center=True, gain_bc=None):
        nc, ctx = self.nc, self.ctx
        c1, c2 = self.col1, self.col2
        self.konst(1.0 / ddim); self.konst(eps); self.konst(-1.0)
        if center:
            ctx.v(lambda: nc.vector.reduce_sum(c1[0:128, 0:1], t_in, axis=AX.X))
            ctx.s(lambda: nc.scalar.mul(c1[0:128, 0:1], c1[0:128, 0:1], 1.0 / ddim))
            ctx.v(lambda: nc.vector.tensor_scalar(t_out, t_in, c1[0:128, 0:1], None, ALU.subtract))
            src = t_out
        else:
            src = t_in
        ctx.s(lambda: nc.scalar.activation(self.sc2[0:128, 0:ddim], src, AF.Square))
        ctx.v(lambda: nc.vector.reduce_sum(c2[0:128, 0:1], self.sc2[0:128, 0:ddim], axis=AX.X))
        ctx.s(lambda: nc.scalar.mul(c2[0:128, 0:1], c2[0:128, 0:1], 1.0 / ddim))
        ctx.s(lambda: nc.scalar.activation(c2[0:128, 0:1], c2[0:128, 0:1], AF.Sqrt, bias=eps))
        ctx.v(lambda: nc.vector.reciprocal(c2[0:128, 0:1], c2[0:128, 0:1]))
        ctx.v(lambda: nc.vector.tensor_scalar(t_out, src, c2[0:128, 0:1], None, ALU.mult))
        if gain_bc is not None:
            ctx.v(lambda: nc.vector.tensor_mul(t_out, t_out, gain_bc))

    def softmax_row(self, t, n):
        nc, ctx = self.nc, self.ctx
        c1 = self.col3
        ctx.v(lambda: nc.vector.reduce_max(c1[0:128, 0:1], t, axis=AX.X))
        ctx.s(lambda: nc.scalar.mul(c1[0:128, 0:1], c1[0:128, 0:1], -1.0))
        ctx.s(lambda: nc.scalar.activation(t, t, AF.Exp, bias=c1[0:128, 0:1]))
        ctx.v(lambda: nc.vector.reduce_sum(c1[0:128, 0:1], t, axis=AX.X))
        ctx.v(lambda: nc.vector.reciprocal(c1[0:128, 0:1], c1[0:128, 0:1]))
        ctx.v(lambda: nc.vector.tensor_scalar(t, t, c1[0:128, 0:1], None, ALU.mult))

    def mm(self, C, A, B, M, K, N, evac_scale=1.0, post=None):
        nc, ctx = self.nc, self.ctx
        self.konst(evac_scale)
        nkc = ceil_div(K, 128)
        nnc = ceil_div(N, 512)
        for mi in range(M // 128):
            m0 = mi * 128
            ctx.dma(self.a_sb[0:128, 0:K], A[m0:m0 + 128, 0:K])
            for ki in range(nkc):
                kk = min(128, K - ki * 128)
                self.transpose_to(self.a_sb[0:128, ki * 128:ki * 128 + kk], kk,
                                  self.at_sb[0:128, ki * 128:ki * 128 + 128])
            for g0 in range(0, nnc, 4):
                gl = list(range(g0, min(g0 + 4, nnc)))
                n0 = g0 * 512
                gw = min(N, (g0 + 4) * 512) - n0
                for ki in range(nkc):
                    kk = min(128, K - ki * 128)
                    k0 = ki * 128
                    ctx.dma(self.b_sb[0:kk, 0:gw], B[k0:k0 + kk, n0:n0 + gw])
                    for ni in gl:
                        nn = min(512, N - ni * 512)
                        off = ni * 512 - n0
                        ctx.t(lambda ki=ki, kk=kk, ni=ni, nn=nn, off=off:
                              nc.tensor.matmul(
                                  self.ps[ni % 4][0:128, 0:nn],
                                  self.at_sb[0:kk, (ki % 32) * 128:(ki % 32) * 128 + 128],
                                  self.b_sb[0:kk, off:off + nn],
                                  start=(ki == 0), stop=(ki == nkc - 1)))
                for ni in gl:
                    nn = min(512, N - ni * 512)
                    ctx.s(lambda ni=ni, nn=nn: nc.scalar.activation(
                        self.row[0:128, ni * 512:ni * 512 + nn],
                        self.ps[ni % 4][0:128, 0:nn], AF.Copy, scale=evac_scale))
            if post is not None:
                post(mi)
            if C is not None:
                ctx.dma(C[m0:m0 + 128, 0:N], self.row[0:128, 0:N])

    def transpose_hbm(self, dst, src, M, N):
        nc, ctx = self.nc, self.ctx
        for mi in range(M // 128):
            for ni in range(N // 128):
                ctx.dma(self.sc1[0:128, 0:128], src[mi * 128:mi * 128 + 128, ni * 128:ni * 128 + 128])
                self.transpose_to(self.sc1[0:128, 0:128], 128, self.sc2[0:128, 0:128])
                ctx.dma(dst[ni * 128:ni * 128 + 128, mi * 128:mi * 128 + 128], self.sc2[0:128, 0:128])

    def ln_hbm(self, dst, src, M, dd, eps=1e-5, center=True, gain_bc=None):
        nc, ctx = self.nc, self.ctx
        for mi in range(M // 128):
            ctx.dma(self.sc1[0:128, 0:dd], src[mi * 128:mi * 128 + 128, 0:dd])
            self.ln(self.sc1[0:128, 0:dd], self.sc1[0:128, 0:dd], dd, eps=eps,
                    center=center, gain_bc=gain_bc)
            ctx.dma(dst[mi * 128:mi * 128 + 128, 0:dd], self.sc1[0:128, 0:dd])


import contextlib

_prog_cache = {}


def _build(name, fn):
    if name not in _prog_cache:
        nc = bass.Bass("TRN2", target_bir_lowering=False)
        ctx = Ctx(nc)
        holder = {}

        def fn2(nc, ctx, es):
            b = fn(nc, ctx, es)
            holder["kmap"] = b

        with contextlib.ExitStack() as stack:
            es = stack.enter_context
            fn2(nc, ctx, es)
            ctx.finalize()
        _prog_cache[name] = (nc, holder["kmap"])
    return _prog_cache[name]



def build_l0attn(nc, ctx, es):
    x = nc.dram_tensor("x", [S, D], F32, kind="ExternalInput").ap()
    wqT = nc.dram_tensor("wqT", [D, D], F32, kind="ExternalInput").ap()
    wkT = nc.dram_tensor("wkT", [D, D], F32, kind="ExternalInput").ap()
    wvT = nc.dram_tensor("wvT", [D, D], F32, kind="ExternalInput").ap()
    woT = nc.dram_tensor("woT", [D, D], F32, kind="ExternalInput").ap()
    idn = nc.dram_tensor("idn", [128, 128], F32, kind="ExternalInput").ap()
    ones_r = nc.dram_tensor("ones_r", [1, 128], F32, kind="ExternalInput").ap()
    kvec = nc.dram_tensor("kvec", [128, 64], F32, kind="ExternalInput").ap()
    x1 = nc.dram_tensor("x1", [S, D], F32, kind="ExternalOutput").ap()
    xln = nc.dram_tensor("xln", [S, D], F32, kind="Internal").ap()
    Q = nc.dram_tensor("Qs", [S, D], F32, kind="Internal").ap()
    Kt = nc.dram_tensor("Ks", [S, D], F32, kind="Internal").ap()
    V = nc.dram_tensor("Vs", [S, D], F32, kind="Internal").ap()
    KT = nc.dram_tensor("KTs", [D, S], F32, kind="Internal").ap()
    P = nc.dram_tensor("Ps", [S, S], F32, kind="Internal").ap()
    Y = nc.dram_tensor("Ys", [S, D], F32, kind="Internal").ap()
    YT = nc.dram_tensor("YTs", [D, S], F32, kind="Internal").ap()

    b = Builder(nc, ctx, es)
    b.load_ident(idn, ones_r, kvec)
    b.ln_hbm(xln, x, S, D)
    b.mm(Q, xln, wqT, S, D, D)
    b.mm(Kt, xln, wkT, S, D, D)
    b.mm(V, xln, wvT, S, D, D)
    b.transpose_hbm(KT, Kt, S, D)

    def post_sm(mi):
        b.softmax_row(b.row[0:128, 0:S], S)

    b.mm(P, Q, KT, S, D, S, evac_scale=1.0 / 8.0, post=post_sm)
    b.mm(Y, P, V, S, S, D)
    b.transpose_hbm(YT, Y, S, D)
    Z = YT.rearrange("a (c d) -> (a c) d", c=2, d=D)

    def post_out(mi):
        b.ln(b.row[0:128, 0:D], b.row[0:128, 0:D], D)
        ctx.dma(b.sc1[0:128, 0:D], x[mi * 128:mi * 128 + 128, 0:D])
        ctx.v(lambda: nc.vector.tensor_add(b.row[0:128, 0:D], b.row[0:128, 0:D], b.sc1[0:128, 0:D]))

    b.mm(x1, Z, woT, S, D, D, post=post_out)
    return b



def build_l0ffn(nc, ctx, es):
    H0 = 4096
    x = nc.dram_tensor("x", [S, D], F32, kind="ExternalInput").ap()
    k0 = nc.dram_tensor("k0", [2 * D, H0], U8, kind="ExternalInput").ap()
    k1 = nc.dram_tensor("k1", [H0, H0], U8, kind="ExternalInput").ap()
    k2 = nc.dram_tensor("k2", [H0, D], U8, kind="ExternalInput").ap()
    s0 = nc.dram_tensor("s0", [1, 1], F32, kind="ExternalInput").ap()
    s1 = nc.dram_tensor("s1", [1, 1], F32, kind="ExternalInput").ap()
    s2 = nc.dram_tensor("s2", [1, 1], F32, kind="ExternalInput").ap()
    idn = nc.dram_tensor("idn", [128, 128], F32, kind="ExternalInput").ap()
    ones_r = nc.dram_tensor("ones_r", [1, 128], F32, kind="ExternalInput").ap()
    kvec = nc.dram_tensor("kvec", [128, 64], F32, kind="ExternalInput").ap()
    x2 = nc.dram_tensor("x2", [S, D], F32, kind="ExternalOutput").ap()
    w0 = nc.dram_tensor("w0", [D, H0], F32, kind="Internal").ap()
    w1 = nc.dram_tensor("w1", [H0, H0], F32, kind="Internal").ap()
    w2 = nc.dram_tensor("w2", [H0, D], F32, kind="Internal").ap()
    xln = nc.dram_tensor("xln", [S, D], F32, kind="Internal").ap()
    h1 = nc.dram_tensor("h1", [S, H0], F32, kind="Internal").ap()
    h2 = nc.dram_tensor("h2", [S, H0], F32, kind="Internal").ap()

    b = Builder(nc, ctx, es)
    b.load_ident(idn, ones_r, kvec)
    b.konst(2.0)
    u8a = es(nc.sbuf_tensor("u8a", [128, H0], U8))
    u8b = es(nc.sbuf_tensor("u8b", [128, H0], U8))
    fa = b.sc1
    fb = b.row
    for i in range(D // 128):
        ctx.dma(u8a[0:128, 0:H0], k0[i * 128:i * 128 + 128, 0:H0])
        ctx.dma(u8b[0:128, 0:H0], k0[D + i * 128:D + i * 128 + 128, 0:H0])
        ctx.v(lambda: nc.vector.tensor_copy(fa[0:128, 0:H0], u8a[0:128, 0:H0]))
        ctx.v(lambda: nc.vector.tensor_copy(fb[0:128, 0:H0], u8b[0:128, 0:H0]))
        ctx.v(lambda: nc.vector.tensor_sub(fa[0:128, 0:H0], fa[0:128, 0:H0], fb[0:128, 0:H0]))
        ctx.s(lambda: nc.scalar.mul(fa[0:128, 0:H0], fa[0:128, 0:H0], 2.0))
        ctx.dma(w0[i * 128:i * 128 + 128, 0:H0], fa[0:128, 0:H0])
    for (src, dst, nn, rows) in ((k1, w1, H0, H0), (k2, w2, D, H0)):
        for i in range(rows // 128):
            ctx.dma(u8a[0:128, 0:nn], src[i * 128:i * 128 + 128, 0:nn])
            ctx.v(lambda nn=nn: nc.vector.tensor_copy(fa[0:128, 0:nn], u8a[0:128, 0:nn]))
            ctx.s(lambda nn=nn: nc.scalar.mul(fa[0:128, 0:nn], fa[0:128, 0:nn], 2.0))
            ctx.s(lambda nn=nn: nc.scalar.add(fa[0:128, 0:nn], fa[0:128, 0:nn], -1.0))
            ctx.dma(dst[i * 128:i * 128 + 128, 0:nn], fa[0:128, 0:nn])
    s0c = es(nc.sbuf_tensor("s0c", [128, 1], F32))
    s1c = es(nc.sbuf_tensor("s1c", [128, 1], F32))
    s2c = es(nc.sbuf_tensor("s2c", [128, 1], F32))
    for (sap, scol) in ((s0, s0c), (s1, s1c), (s2, s2c)):
        ctx.dma(b.tiny[0:1, 0:1], sap[0:1, 0:1])
        ctx.t(lambda: nc.tensor.matmul(b.psb[0:128, 0:1], b.ones[0:1, 0:128],
                                       b.tiny[0:1, 0:1], start=True, stop=True))
        ctx.s(lambda scol=scol: nc.scalar.copy(scol[0:128, 0:1], b.psb[0:128, 0:1]))

    b.ln_hbm(xln, x, S, D)

    def post_relu(scol):
        def post(mi):
            ctx.v(lambda: nc.vector.tensor_scalar(b.row[0:128, 0:H0], b.row[0:128, 0:H0],
                                                  scol[0:128, 0:1], None, ALU.mult))
            ctx.s(lambda: nc.scalar.activation(b.row[0:128, 0:H0], b.row[0:128, 0:H0],
                                               AF.Relu, scale=2.0))
        return post

    b.mm(h1, xln, w0, S, D, H0, post=post_relu(s0c))
    b.mm(h2, h1, w1, S, H0, H0, post=post_relu(s1c))

    def post_fin(mi):
        ctx.v(lambda: nc.vector.tensor_scalar(b.row[0:128, 0:D], b.row[0:128, 0:D],
                                              s2c[0:128, 0:1], None, ALU.mult))
        b.ln(b.row[0:128, 0:D], b.row[0:128, 0:D], D)
        ctx.dma(b.sc1[0:128, 0:D], x[mi * 128:mi * 128 + 128, 0:D])
        ctx.v(lambda: nc.vector.tensor_add(b.row[0:128, 0:D], b.row[0:128, 0:D], b.sc1[0:128, 0:D]))

    b.mm(x2, h2, w2, S, H0, D, post=post_fin)
    return b



def build_l1pre(nc, ctx, es):
    T = 2 * S
    x = nc.dram_tensor("x", [T, D], F32, kind="ExternalInput").ap()
    WkT = nc.dram_tensor("WkT", [D, D], F32, kind="ExternalInput").ap()
    bk = nc.dram_tensor("bk", [1, D], F32, kind="ExternalInput").ap()
    idn = nc.dram_tensor("idn", [128, 128], F32, kind="ExternalInput").ap()
    ones_r = nc.dram_tensor("ones_r", [1, 128], F32, kind="ExternalInput").ap()
    kvec = nc.dram_tensor("kvec", [128, 64], F32, kind="ExternalInput").ap()
    lmax = nc.dram_tensor("lmax", [1, 1], F32, kind="ExternalOutput").ap()
    xln = nc.dram_tensor("xln", [T, D], F32, kind="Internal").ap()
    Kt = nc.dram_tensor("Ks", [T, D], F32, kind="Internal").ap()

    b = Builder(nc, ctx, es)
    b.load_ident(idn, ones_r, kvec)
    b.ln_hbm(xln, x, T, D)
    bkbc = es(nc.sbuf_tensor("bkbc", [128, D], F32))
    b.bc_row(bk, D, bkbc)
    acc = es(nc.sbuf_tensor("accm", [128, 1], F32))
    d16 = es(nc.sbuf_tensor("d16", [128, 16], F32))
    b.konst(-0.5)
    ctx.s(lambda: nc.scalar.copy(acc[0:128, 0:1], b.kcol(-1e30)))

    def post(mi):
        ctx.v(lambda: nc.vector.tensor_add(b.row[0:128, 0:D], b.row[0:128, 0:D], bkbc[0:128, 0:D]))
        ctx.s(lambda: nc.scalar.activation(b.sc1[0:128, 0:D], b.row[0:128, 0:D], AF.Square))
        for h in range(16):
            ctx.v(lambda h=h: nc.vector.reduce_sum(d16[0:128, h:h + 1],
                                                   b.sc1[0:128, h * 64:h * 64 + 64], axis=AX.X))
        ctx.s(lambda: nc.scalar.mul(d16[0:128, 0:16], d16[0:128, 0:16], -0.5))
        ctx.v(lambda: nc.vector.reduce_max(b.col3[0:128, 0:1], d16[0:128, 0:16], axis=AX.X))
        ctx.v(lambda: nc.vector.tensor_tensor(acc[0:128, 0:1], acc[0:128, 0:1],
                                              b.col3[0:128, 0:1], ALU.max))

    b.mm(Kt, xln, WkT, T, D, D, post=post)
    ctx.t(lambda: nc.tensor.transpose(b.pst[0:1, 0:128], acc[0:128, 0:1], b.ident[:, :]))
    ctx.s(lambda: nc.scalar.copy(b.tiny[0:1, 0:128], b.pst[0:1, 0:128]))
    ctx.v(lambda: nc.vector.reduce_max(b.col3[0:1, 0:1], b.tiny[0:1, 0:128], axis=AX.X))
    ctx.dma(lmax[0:1, 0:1], b.col3[0:1, 0:1])
    return b



def build_l1attn(nc, ctx, es):
    M = 256
    x = nc.dram_tensor("x", [S, D], F32, kind="ExternalInput").ap()
    WqT = nc.dram_tensor("WqT", [D, D], F32, kind="ExternalInput").ap()
    WkT = nc.dram_tensor("WkT", [D, D], F32, kind="ExternalInput").ap()
    WvT = nc.dram_tensor("WvT", [D, D], F32, kind="ExternalInput").ap()
    WoT = nc.dram_tensor("WoT", [D, D], F32, kind="ExternalInput").ap()
    bq = nc.dram_tensor("bq", [1, D], F32, kind="ExternalInput").ap()
    bk = nc.dram_tensor("bk", [1, D], F32, kind="ExternalInput").ap()
    bv = nc.dram_tensor("bv", [1, D], F32, kind="ExternalInput").ap()
    bo = nc.dram_tensor("bo", [1, D], F32, kind="ExternalInput").ap()
    projT = nc.dram_tensor("projT", [64, M], F32, kind="ExternalInput").ap()
    lmax8 = nc.dram_tensor("lmax8", [1, 8], F32, kind="ExternalInput").ap()
    idn = nc.dram_tensor("idn", [128, 128], F32, kind="ExternalInput").ap()
    ones_r = nc.dram_tensor("ones_r", [1, 128], F32, kind="ExternalInput").ap()
    kvec = nc.dram_tensor("kvec", [128, 64], F32, kind="ExternalInput").ap()
    x3 = nc.dram_tensor("x3", [S, D], F32, kind="ExternalOutput").ap()
    xln = nc.dram_tensor("xln", [S, D], F32, kind="Internal").ap()
    Q = nc.dram_tensor("Qs", [S, D], F32, kind="Internal").ap()
    Kt = nc.dram_tensor("Ks", [S, D], F32, kind="Internal").ap()
    V = nc.dram_tensor("Vs", [S, D], F32, kind="Internal").ap()
    Qf = nc.dram_tensor("Qf", [S, M], F32, kind="Internal").ap()
    Kf = nc.dram_tensor("Kf", [S, M], F32, kind="Internal").ap()
    OutP = nc.dram_tensor("OutP", [S, D], F32, kind="Internal").ap()

    b = Builder(nc, ctx, es)
    b.load_ident(idn, ones_r, kvec)
    KEPS = 0.001
    MS = M ** (-0.5)
    nrm = 64 ** (-0.25)

    bqbc = es(nc.sbuf_tensor("bqbc", [128, D], F32))
    bkbc = es(nc.sbuf_tensor("bkbc", [128, D], F32))
    bvbc = es(nc.sbuf_tensor("bvbc", [128, D], F32))
    bobc = es(nc.sbuf_tensor("bobc", [128, D], F32))
    b.bc_row(bq, D, bqbc)
    b.bc_row(bk, D, bkbc)
    b.bc_row(bv, D, bvbc)
    b.bc_row(bo, D, bobc)
    nst = es(nc.sbuf_tensor("nst", [128, 1], F32))
    ctx.dma(b.tiny[0:1, 0:8], lmax8[0:1, 0:8])
    ctx.v(lambda: nc.vector.reduce_max(b.tiny[0:1, 8:9], b.tiny[0:1, 0:8], axis=AX.X))
    ctx.t(lambda: nc.tensor.matmul(b.psb[0:128, 0:1], b.ones[0:1, 0:128],
                                   b.tiny[0:1, 8:9], start=True, stop=True))
    ctx.s(lambda: nc.scalar.activation(nst[0:128, 0:1], b.psb[0:128, 0:1], AF.Copy, scale=-1.0))

    b.ln_hbm(xln, x, S, D)

    def mkpost(bc):
        def post(mi):
            ctx.v(lambda: nc.vector.tensor_add(b.row[0:128, 0:D], b.row[0:128, 0:D], bc[0:128, 0:D]))
        return post

    b.mm(Q, xln, WqT, S, D, D, post=mkpost(bqbc))
    b.mm(Kt, xln, WkT, S, D, D, post=mkpost(bkbc))
    b.mm(V, xln, WvT, S, D, D, post=mkpost(bvbc))

    b.konst(KEPS); b.konst(M ** (-0.5)); b.konst(64 ** (-0.25)); b.konst(-0.5)
    kf_sb = es(nc.sbuf_tensor("kf_sb", [128, M], F32))
    vx_sb = es(nc.sbuf_tensor("vx_sb", [128, 65], F32))
    cx_sb = es(nc.sbuf_tensor("cx_sb", [128, 130], F32))
    qt_sb = es(nc.sbuf_tensor("qt_sb", [128, 64], F32))
    pso = es(nc.psum_tensor("pso", [128, 128], F32))

    for h in range(16):
        c0 = h * 64

        def post_qf(mi):
            ctx.s(lambda: nc.scalar.activation(b.row[0:128, 0:M], b.row[0:128, 0:M], AF.Exp))
            ctx.s(lambda: nc.scalar.add(b.row[0:128, 0:M], b.row[0:128, 0:M], KEPS))
            ctx.s(lambda: nc.scalar.mul(b.row[0:128, 0:M], b.row[0:128, 0:M], MS))

        def post_kf(mi):
            ctx.dma(qt_sb[0:128, 0:64], Kt[mi * 128:mi * 128 + 128, c0:c0 + 64])
            ctx.s(lambda: nc.scalar.activation(b.sc2[0:128, 0:64], qt_sb[0:128, 0:64], AF.Square))
            ctx.v(lambda: nc.vector.reduce_sum(b.col3[0:128, 0:1], b.sc2[0:128, 0:64], axis=AX.X))
            ctx.s(lambda: nc.scalar.mul(b.col3[0:128, 0:1], b.col3[0:128, 0:1], -0.5))
            ctx.v(lambda: nc.vector.tensor_add(b.col3[0:128, 0:1], b.col3[0:128, 0:1], nst[0:128, 0:1]))
            ctx.s(lambda: nc.scalar.activation(b.row[0:128, 0:M], b.row[0:128, 0:M],
                                               AF.Exp, bias=b.col3[0:128, 0:1]))
            ctx.s(lambda: nc.scalar.add(b.row[0:128, 0:M], b.row[0:128, 0:M], KEPS))
            ctx.s(lambda: nc.scalar.mul(b.row[0:128, 0:M], b.row[0:128, 0:M], MS))

        b.mm(Qf, Q[0:S, c0:c0 + 64], projT, S, 64, M,
             evac_scale=nrm, post=post_qf)
        b.mm(Kf, Kt[0:S, c0:c0 + 64], projT, S, 64, M,
             evac_scale=nrm, post=post_kf)
        for f in range(2):
            for ti in range(S // 128):
                t0 = ti * 128
                ctx.dma(kf_sb[0:128, 0:M], Kf[t0:t0 + 128, 0:M])
                ctx.dma(vx_sb[0:128, 0:64], V[t0:t0 + 128, c0:c0 + 64])
                ctx.s(lambda: nc.scalar.copy(vx_sb[0:128, 64:65], b.kcol(1.0)))
                ctx.t(lambda f=f, ti=ti: nc.tensor.matmul(
                    b.ps[f][0:128, 0:65], kf_sb[0:128, f * 128:f * 128 + 128],
                    vx_sb[0:128, 0:65], start=(ti == 0), stop=(ti == S // 128 - 1)))
            ctx.s(lambda f=f: nc.scalar.copy(cx_sb[0:128, f * 65:f * 65 + 65], b.ps[f][0:128, 0:65]))
        for mi in range(S // 128):
            m0 = mi * 128
            ctx.dma(b.sc1[0:128, 0:M], Qf[m0:m0 + 128, 0:M])
            for f in range(2):
                b.transpose_to(b.sc1[0:128, f * 128:f * 128 + 128], 128,
                               b.at_sb[0:128, f * 128:f * 128 + 128])
            for f in range(2):
                ctx.t(lambda f=f: nc.tensor.matmul(
                    pso[0:128, 0:65], b.at_sb[0:128, f * 128:f * 128 + 128],
                    cx_sb[0:128, f * 65:f * 65 + 65], start=(f == 0), stop=(f == 1)))
            ctx.v(lambda: nc.vector.reciprocal(b.col3[0:128, 0:1], pso[0:128, 64:65]))
            ctx.v(lambda: nc.vector.tensor_scalar(b.sc2[0:128, 0:64], pso[0:128, 0:64],
                                                  b.col3[0:128, 0:1], None, ALU.mult))
            ctx.dma(OutP[m0:m0 + 128, c0:c0 + 64], b.sc2[0:128, 0:64])

    def post_out(mi):
        ctx.v(lambda: nc.vector.tensor_add(b.row[0:128, 0:D], b.row[0:128, 0:D], bobc[0:128, 0:D]))
        b.ln(b.row[0:128, 0:D], b.row[0:128, 0:D], D)
        ctx.dma(b.sc1[0:128, 0:D], x[mi * 128:mi * 128 + 128, 0:D])
        ctx.v(lambda: nc.vector.tensor_add(b.row[0:128, 0:D], b.row[0:128, 0:D], b.sc1[0:128, 0:D]))

    b.mm(x3, OutP, WoT, S, D, D, post=post_out)
    return b



def build_l1fattn(nc, ctx, es):
    x = nc.dram_tensor("x", [S, D], F32, kind="ExternalInput").ap()
    wqT = nc.dram_tensor("wqT", [D, D], F32, kind="ExternalInput").ap()
    wkT = nc.dram_tensor("wkT", [D, D], F32, kind="ExternalInput").ap()
    wv = nc.dram_tensor("wv", [D, D], F32, kind="ExternalInput").ap()
    wo = nc.dram_tensor("wo", [D, D], F32, kind="ExternalInput").ap()
    g1 = nc.dram_tensor("g1", [1, D], F32, kind="ExternalInput").ap()
    cosT = nc.dram_tensor("cosT", [D, S], F32, kind="ExternalInput").ap()
    sinT = nc.dram_tensor("sinT", [D, S], F32, kind="ExternalInput").ap()
    msk = nc.dram_tensor("msk", [4 * 128, 512], F32, kind="ExternalInput").ap()
    idn = nc.dram_tensor("idn", [128, 128], F32, kind="ExternalInput").ap()
    ones_r = nc.dram_tensor("ones_r", [1, 128], F32, kind="ExternalInput").ap()
    kvec = nc.dram_tensor("kvec", [128, 64], F32, kind="ExternalInput").ap()
    hout = nc.dram_tensor("hout", [S, D], F32, kind="ExternalOutput").ap()
    xl4o = nc.dram_tensor("xl4o", [S, D], F32, kind="ExternalOutput").ap()
    xln4 = nc.dram_tensor("xln4", [S, D], F32, kind="Internal").ap()
    rn1 = nc.dram_tensor("rn1", [S, D], F32, kind="Internal").ap()
    rnT = nc.dram_tensor("rnT", [D, S], F32, kind="Internal").ap()
    QT = nc.dram_tensor("QTs", [D, S], F32, kind="Internal").ap()
    KT = nc.dram_tensor("KTs", [D, S], F32, kind="Internal").ap()
    V = nc.dram_tensor("Vs", [S, D], F32, kind="Internal").ap()
    AttO = nc.dram_tensor("AttO", [S, D], F32, kind="Internal").ap()

    b = Builder(nc, ctx, es)
    b.load_ident(idn, ones_r, kvec)
    g1bc = es(nc.sbuf_tensor("g1bc", [128, D], F32))
    b.bc_row(g1, D, g1bc)
    b.ln_hbm(xln4, x, S, D)
    for mi in range(S // 128):
        ctx.dma(b.sc1[0:128, 0:D], xln4[mi * 128:mi * 128 + 128, 0:D])
        ctx.dma(xl4o[mi * 128:mi * 128 + 128, 0:D], b.sc1[0:128, 0:D])
    b.ln_hbm(rn1, xln4, S, D, eps=1e-6, center=False, gain_bc=g1bc[0:128, 0:1024])
    b.transpose_hbm(rnT, rn1, S, D)
    b.mm(QT, wqT, rnT, D, D, S)
    b.mm(KT, wkT, rnT, D, D, S)
    b.mm(V, rn1, wv, S, D, D)
    ct_sb = es(nc.sbuf_tensor("ct_sb", [128, S], F32))
    st_sb = es(nc.sbuf_tensor("st_sb", [128, S], F32))
    qt_sb = es(nc.sbuf_tensor("qt2_sb", [128, S], F32))
    rt_sb = es(nc.sbuf_tensor("rt_sb", [128, S], F32))
    for src in (QT, KT):
        for i in range(D // 128):
            r0 = i * 128
            ctx.dma(ct_sb[0:128, 0:S], cosT[r0:r0 + 128, 0:S])
            ctx.dma(st_sb[0:128, 0:S], sinT[r0:r0 + 128, 0:S])
            ctx.dma(qt_sb[0:128, 0:S], src[r0:r0 + 128, 0:S])
            for g in range(2):
                p0 = g * 64
                ctx.s(lambda p0=p0: nc.scalar.activation(rt_sb[p0:p0 + 32, 0:S],
                                                         qt_sb[p0 + 32:p0 + 64, 0:S], AF.Copy, scale=-1.0))
                ctx.s(lambda p0=p0: nc.scalar.copy(rt_sb[p0 + 32:p0 + 64, 0:S],
                                                   qt_sb[p0:p0 + 32, 0:S]))
            ctx.v(lambda: nc.vector.tensor_mul(qt_sb[0:128, 0:S], qt_sb[0:128, 0:S], ct_sb[0:128, 0:S]))
            ctx.v(lambda: nc.vector.tensor_mul(rt_sb[0:128, 0:S], rt_sb[0:128, 0:S], st_sb[0:128, 0:S]))
            ctx.v(lambda: nc.vector.tensor_add(qt_sb[0:128, 0:S], qt_sb[0:128, 0:S], rt_sb[0:128, 0:S]))
            ctx.dma(src[r0:r0 + 128, 0:S], qt_sb[0:128, 0:S])
    mk_sb = es(nc.sbuf_tensor("mk_sb", [128, 4 * 512], F32))
    for o in range(4):
        ctx.dma(mk_sb[0:128, o * 512:o * 512 + 512], msk[o * 128:o * 128 + 128, 0:512])
    NKT = S // 128
    pstl = [es(nc.sbuf_tensor("pstl%d" % i, [128, 512], F32)) for i in range(NKT)]
    vxl = [es(nc.sbuf_tensor("vxl%d" % i, [128, 65], F32)) for i in range(NKT)]
    lt_sb = es(nc.sbuf_tensor("lt_sb", [64, 128], F32))
    rq_sb = es(nc.sbuf_tensor("rq_sb", [64, 512], F32))
    pso = es(nc.psum_tensor("pso", [128, 128], F32))
    scale = 64 ** (-0.5)
    b.konst(scale)
    for h in range(16):
        c0 = h * 64
        for kt in range(NKT):
            ctx.dma(vxl[kt][0:128, 0:64], V[kt * 128:kt * 128 + 128, c0:c0 + 64])
            ctx.s(lambda kt=kt: nc.scalar.copy(vxl[kt][0:128, 64:65], b.kcol(1.0)))
        for qn in range(S // 512):
            q0 = qn * 512
            nkts = (q0 + 512) // 128
            ctx.dma(rq_sb[0:64, 0:512], QT[c0:c0 + 64, q0:q0 + 512])
            for kt in range(nkts):
                ctx.dma(lt_sb[0:64, 0:128], KT[c0:c0 + 64, kt * 128:kt * 128 + 128])
                ctx.t(lambda kt=kt: nc.tensor.matmul(
                    b.ps[0][0:128, 0:512], lt_sb[0:64, 0:128], rq_sb[0:64, 0:512],
                    start=True, stop=True))
                ctx.s(lambda kt=kt: nc.scalar.activation(pstl[kt][0:128, 0:512],
                                                         b.ps[0][0:128, 0:512], AF.Exp, scale=scale))
                off = kt * 128 - q0
                if off >= 0:
                    oi = off // 128
                    ctx.v(lambda kt=kt, oi=oi: nc.vector.tensor_mul(
                        pstl[kt][0:128, 0:512], pstl[kt][0:128, 0:512],
                        mk_sb[0:128, oi * 512:oi * 512 + 512]))
            for j in range(4):
                qt0 = q0 + j * 128
                nk2 = (qt0 + 128) // 128
                for kt in range(nk2):
                    ctx.t(lambda kt=kt, j=j, nk2=nk2: nc.tensor.matmul(
                        pso[0:128, 0:65], pstl[kt][0:128, j * 128:j * 128 + 128],
                        vxl[kt][0:128, 0:65], start=(kt == 0), stop=(kt == nk2 - 1)))
                ctx.v(lambda: nc.vector.reciprocal(b.col3[0:128, 0:1], pso[0:128, 64:65]))
                ctx.v(lambda: nc.vector.tensor_scalar(b.sc2[0:128, 0:64], pso[0:128, 0:64],
                                                      b.col3[0:128, 0:1], None, ALU.mult))
                ctx.dma(AttO[qt0:qt0 + 128, c0:c0 + 64], b.sc2[0:128, 0:64])

    def post_h(mi):
        ctx.dma(b.sc1[0:128, 0:D], xln4[mi * 128:mi * 128 + 128, 0:D])
        ctx.v(lambda: nc.vector.tensor_add(b.row[0:128, 0:D], b.row[0:128, 0:D], b.sc1[0:128, 0:D]))

    b.mm(hout, AttO, wo, S, D, D, post=post_h)
    return b



def build_router(nc, ctx, es):
    T = 2 * S
    NE = 8
    h = nc.dram_tensor("h", [T, D], F32, kind="ExternalInput").ap()
    g2 = nc.dram_tensor("g2", [1, D], F32, kind="ExternalInput").ap()
    wr = nc.dram_tensor("wr", [D, NE], F32, kind="ExternalInput").ap()
    idn = nc.dram_tensor("idn", [128, 128], F32, kind="ExternalInput").ap()
    ones_r = nc.dram_tensor("ones_r", [1, 128], F32, kind="ExternalInput").ap()
    kvec = nc.dram_tensor("kvec", [128, 64], F32, kind="ExternalInput").ap()
    XF = nc.dram_tensor("XF", [T, D], F32, kind="ExternalOutput").ap()
    COMB = nc.dram_tensor("COMB", [T, NE], F32, kind="ExternalOutput").ap()

    b = Builder(nc, ctx, es)
    b.load_ident(idn, ones_r, kvec)
    g2bc = es(nc.sbuf_tensor("g2bc", [128, D], F32))
    b.bc_row(g2, D, g2bc)
    b.ln_hbm(XF, h, T, D, eps=1e-6, center=False, gain_bc=g2bc[0:128, 0:1024])

    def post(mi):
        p = b.row
        b.softmax_row(p[0:128, 0:NE], NE)
        c1, c2 = b.col1, b.col2
        ctx.v(lambda: nc.vector.reduce_max(c1[0:128, 0:1], p[0:128, 0:NE], axis=AX.X))
        ctx.v(lambda: nc.vector.tensor_scalar(b.sc2[0:128, 0:NE], p[0:128, 0:NE],
                                              c1[0:128, 0:1], None, ALU.is_equal))
        ctx.v(lambda: nc.vector.tensor_mul(b.sc2[0:128, 0:NE], b.sc2[0:128, 0:NE], p[0:128, 0:NE]))
        ctx.v(lambda: nc.vector.tensor_sub(b.sc2[0:128, 0:NE], p[0:128, 0:NE], b.sc2[0:128, 0:NE]))
        ctx.v(lambda: nc.vector.reduce_max(c2[0:128, 0:1], b.sc2[0:128, 0:NE], axis=AX.X))
        ctx.v(lambda: nc.vector.tensor_scalar(b.sc2[0:128, 0:NE], p[0:128, 0:NE],
                                              c2[0:128, 0:1], None, ALU.is_ge))
        ctx.v(lambda: nc.vector.tensor_mul(p[0:128, 0:NE], p[0:128, 0:NE], b.sc2[0:128, 0:NE]))
        ctx.v(lambda: nc.vector.tensor_add(c2[0:128, 0:1], c2[0:128, 0:1], c1[0:128, 0:1]))
        ctx.v(lambda: nc.vector.reciprocal(c2[0:128, 0:1], c2[0:128, 0:1]))
        ctx.v(lambda: nc.vector.tensor_scalar(p[0:128, 0:NE], p[0:128, 0:NE],
                                              c2[0:128, 0:1], None, ALU.mult))

    b.mm(COMB, XF, wr, T, D, NE, post=post)
    return b



def build_moe(nc, ctx, es):
    T = 2 * S
    HID = 2752
    xf = nc.dram_tensor("xf", [T, D], F32, kind="ExternalInput").ap()
    wg = nc.dram_tensor("wg", [D, HID], F32, kind="ExternalInput").ap()
    wu = nc.dram_tensor("wu", [D, HID], F32, kind="ExternalInput").ap()
    wd = nc.dram_tensor("wd", [HID, D], F32, kind="ExternalInput").ap()
    ccol = nc.dram_tensor("ccol", [T, 1], F32, kind="ExternalInput").ap()
    accin = nc.dram_tensor("accin", [T, D], F32, kind="ExternalInput").ap()
    idn = nc.dram_tensor("idn", [128, 128], F32, kind="ExternalInput").ap()
    ones_r = nc.dram_tensor("ones_r", [1, 128], F32, kind="ExternalInput").ap()
    kvec = nc.dram_tensor("kvec", [128, 64], F32, kind="ExternalInput").ap()
    accout = nc.dram_tensor("accout", [T, D], F32, kind="ExternalOutput").ap()
    G = nc.dram_tensor("G", [T, HID], F32, kind="Internal").ap()

    b = Builder(nc, ctx, es)
    b.load_ident(idn, ones_r, kvec)
    u_sb = es(nc.sbuf_tensor("u_sb", [128, HID], F32))
    cc_sb = es(nc.sbuf_tensor("cc_sb", [128, 1], F32))

    nkc = D // 128
    nnc = ceil_div(HID, 512)
    for mi in range(T // 128):
        m0 = mi * 128
        ctx.dma(b.a_sb[0:128, 0:D], xf[m0:m0 + 128, 0:D])
        for ki in range(nkc):
            b.transpose_to(b.a_sb[0:128, ki * 128:ki * 128 + 128], 128,
                           b.at_sb[0:128, ki * 128:ki * 128 + 128])
        for (W, dst, act) in ((wg, b.row, AF.Silu), (wu, u_sb, AF.Copy)):
            for g0 in range(0, nnc, 4):
                gl = list(range(g0, min(g0 + 4, nnc)))
                n0 = g0 * 512
                gw = min(HID, (g0 + 4) * 512) - n0
                for ki in range(nkc):
                    ctx.dma(b.b_sb[0:128, 0:gw], W[ki * 128:ki * 128 + 128, n0:n0 + gw])
                    for ni in gl:
                        nn = min(512, HID - ni * 512)
                        off = ni * 512 - n0
                        ctx.t(lambda ki=ki, ni=ni, nn=nn, off=off: nc.tensor.matmul(
                            b.ps[ni % 4][0:128, 0:nn], b.at_sb[0:128, ki * 128:ki * 128 + 128],
                            b.b_sb[0:128, off:off + nn], start=(ki == 0), stop=(ki == nkc - 1)))
                for ni in gl:
                    nn = min(512, HID - ni * 512)
                    ctx.s(lambda ni=ni, nn=nn, dst=dst, act=act: nc.scalar.activation(
                        dst[0:128, ni * 512:ni * 512 + nn], b.ps[ni % 4][0:128, 0:nn], act))
        ctx.v(lambda: nc.vector.tensor_mul(b.row[0:128, 0:HID], b.row[0:128, 0:HID],
                                           u_sb[0:128, 0:HID]))
        ctx.dma(G[m0:m0 + 128, 0:HID], b.row[0:128, 0:HID])
    nkc2 = ceil_div(HID, 128)
    for mi in range(T // 128):
        m0 = mi * 128
        ctx.dma(b.a_sb[0:128, 0:HID], G[m0:m0 + 128, 0:HID])
        for ki in range(nkc2):
            kk = min(128, HID - ki * 128)
            b.transpose_to(b.a_sb[0:128, ki * 128:ki * 128 + kk], kk,
                           b.at_sb[0:128, ki * 128:ki * 128 + 128])
        for ki in range(nkc2):
            kk = min(128, HID - ki * 128)
            ctx.dma(b.b_sb[0:kk, 0:D], wd[ki * 128:ki * 128 + kk, 0:D])
            for ni in range(2):
                ctx.t(lambda ki=ki, kk=kk, ni=ni: nc.tensor.matmul(
                    b.ps[ni][0:128, 0:512], b.at_sb[0:kk, ki * 128:ki * 128 + 128],
                    b.b_sb[0:kk, ni * 512:ni * 512 + 512],
                    start=(ki == 0), stop=(ki == nkc2 - 1)))
        ctx.dma(cc_sb[0:128, 0:1], ccol[m0:m0 + 128, 0:1])
        ctx.dma(b.sc1[0:128, 0:D], accin[m0:m0 + 128, 0:D])
        for ni in range(2):
            ctx.v(lambda ni=ni: nc.vector.scalar_tensor_tensor(
                b.sc1[0:128, ni * 512:ni * 512 + 512], b.ps[ni][0:128, 0:512],
                cc_sb[0:128, 0:1], b.sc1[0:128, ni * 512:ni * 512 + 512],
                ALU.mult, ALU.add))
        ctx.dma(accout[m0:m0 + 128, 0:D], b.sc1[0:128, 0:D])
    return b



def build_final(nc, ctx, es):
    T = 2 * S
    x3 = nc.dram_tensor("x3", [T, D], F32, kind="ExternalInput").ap()
    hh = nc.dram_tensor("hh", [T, D], F32, kind="ExternalInput").ap()
    acc = nc.dram_tensor("acc", [T, D], F32, kind="ExternalInput").ap()
    idn = nc.dram_tensor("idn", [128, 128], F32, kind="ExternalInput").ap()
    ones_r = nc.dram_tensor("ones_r", [1, 128], F32, kind="ExternalInput").ap()
    kvec = nc.dram_tensor("kvec", [128, 64], F32, kind="ExternalInput").ap()
    x4 = nc.dram_tensor("x4", [T, D], F32, kind="ExternalOutput").ap()
    b = Builder(nc, ctx, es)
    b.load_ident(idn, ones_r, kvec)
    for mi in range(T // 128):
        m0 = mi * 128
        ctx.dma(b.sc1[0:128, 0:D], hh[m0:m0 + 128, 0:D])
        ctx.dma(b.row[0:128, 0:D], acc[m0:m0 + 128, 0:D])
        ctx.v(lambda: nc.vector.tensor_add(b.row[0:128, 0:D], b.row[0:128, 0:D], b.sc1[0:128, 0:D]))
        b.ln(b.row[0:128, 0:D], b.row[0:128, 0:D], D)
        ctx.dma(b.sc1[0:128, 0:D], x3[m0:m0 + 128, 0:D])
        ctx.v(lambda: nc.vector.tensor_add(b.row[0:128, 0:D], b.row[0:128, 0:D], b.sc1[0:128, 0:D]))
        ctx.dma(x4[m0:m0 + 128, 0:D], b.row[0:128, 0:D])
    return b



def _run(prog, maps):
    nc, b = prog
    kv = np.zeros((128, 64), dtype=np.float32)
    for v, i in b.kmap.items():
        kv[:, i] = v
    kv = np.ascontiguousarray(kv)
    ones_r = np.ones((1, 128), dtype=np.float32)
    for m in maps:
        m["ones_r"] = ones_r
        m["kvec"] = kv
    res = run_bass_kernel_spmd(nc, maps, core_ids=list(range(NCORE)))
    return res.results


def kernel(**inp):
    f32 = np.float32
    c = np.ascontiguousarray
    idn = np.eye(128, dtype=f32)
    B = 16

    x = c(inp["L0_attn_q"].astype(f32, copy=False))

    ncp = _build("l0attn", build_l0attn)
    wq = {k: c(inp["L0_attn_" + k].T) for k in ("wq", "wk", "wv", "wo")}
    x1 = np.empty_like(x)
    for l in range(2):
        maps = [dict(x=c(x[2 * ci + l]), wqT=wq["wq"], wkT=wq["wk"], wvT=wq["wv"],
                     woT=wq["wo"], idn=idn) for ci in range(NCORE)]
        rr = _run(ncp, maps)
        for ci in range(NCORE):
            x1[2 * ci + l] = rr[ci]["x1"]

    ncp = _build("l0ffn", build_l0ffn)
    k0 = c(inp["L0_ffn_k0"].view(np.uint8))
    k1 = c(inp["L0_ffn_k1"].view(np.uint8))
    k2 = c(inp["L0_ffn_k2"].view(np.uint8))
    s0 = c(np.asarray(inp["L0_ffn_s0"], dtype=f32).reshape(1, 1))
    s1 = c(np.asarray(inp["L0_ffn_s1"], dtype=f32).reshape(1, 1))
    s2 = c(np.asarray(inp["L0_ffn_s2"], dtype=f32).reshape(1, 1))
    x2 = np.empty_like(x)
    for l in range(2):
        maps = [dict(x=c(x1[2 * ci + l]), k0=k0, k1=k1, k2=k2, s0=s0, s1=s1, s2=s2,
                     idn=idn) for ci in range(NCORE)]
        rr = _run(ncp, maps)
        for ci in range(NCORE):
            x2[2 * ci + l] = rr[ci]["x2"]

    ncp = _build("l1pre", build_l1pre)
    WkT = c(inp["L1_attn_Wk"])
    WkT = c(inp["L1_attn_Wk"].T.copy())
    bkr = c(inp["L1_attn_bk"].reshape(1, D).astype(f32))
    maps = [dict(x=c(x2[2 * ci:2 * ci + 2].reshape(2 * S, D)), WkT=WkT, bk=bkr, idn=idn)
            for ci in range(NCORE)]
    rr = _run(ncp, maps)
    lmax8 = c(np.stack([rr[ci]["lmax"].reshape(()) for ci in range(NCORE)]).reshape(1, 8))

    ncp = _build("l1attn", build_l1attn)
    Wt = {k: c(inp["L1_attn_W" + k].T.copy()) for k in ("q", "k", "v", "o")}
    br = {k: c(inp["L1_attn_b" + k].reshape(1, D).astype(f32)) for k in ("q", "k", "v", "o")}
    projT = c(inp["L1_attn_proj"].T.copy())
    x3 = np.empty_like(x)
    for l in range(2):
        maps = [dict(x=c(x2[2 * ci + l]), WqT=Wt["q"], WkT=Wt["k"], WvT=Wt["v"],
                     WoT=Wt["o"], bq=br["q"], bk=br["k"], bv=br["v"], bo=br["o"],
                     projT=projT, lmax8=lmax8, idn=idn) for ci in range(NCORE)]
        rr = _run(ncp, maps)
        for ci in range(NCORE):
            x3[2 * ci + l] = rr[ci]["x3"]

    ncp = _build("l1fattn", build_l1fattn)
    dim = 64
    inv_freq = 1.0 / (10000.0 ** (np.arange(0, dim, 2, dtype=np.float32) / dim))
    tpos = np.arange(S, dtype=np.float32)
    freqs = np.einsum("i,j->ij", tpos, inv_freq)
    emb = np.concatenate([freqs, freqs], axis=-1)
    cosf = np.tile(np.cos(emb), (1, 16))
    sinf = np.tile(np.sin(emb), (1, 16))
    cosT = c(cosf.T.astype(f32))
    sinT = c(sinf.T.astype(f32))
    mk = np.zeros((4, 128, 512), dtype=f32)
    for o in range(4):
        for ll in range(128):
            mk[o, ll, :] = (np.arange(512) >= (ll + o * 128)).astype(f32)
    mk = c(mk.reshape(4 * 128, 512))
    wqT2 = c(inp["L1_ffn_wq"].T.copy())
    wkT2 = c(inp["L1_ffn_wk"].T.copy())
    wv2 = c(inp["L1_ffn_wv"])
    wo2 = c(inp["L1_ffn_wo"])
    g1r = c(inp["L1_ffn_g1"].reshape(1, D).astype(f32))
    hbuf = np.empty_like(x)
    xl4 = np.empty_like(x)
    for l in range(2):
        maps = [dict(x=c(x3[2 * ci + l]), wqT=wqT2, wkT=wkT2, wv=wv2, wo=wo2, g1=g1r,
                     cosT=cosT, sinT=sinT, msk=mk, idn=idn) for ci in range(NCORE)]
        rr = _run(ncp, maps)
        for ci in range(NCORE):
            hbuf[2 * ci + l] = rr[ci]["hout"]
            xl4[2 * ci + l] = rr[ci]["xl4o"]

    ncp = _build("router", build_router)
    g2r = c(inp["L1_ffn_g2"].reshape(1, D).astype(f32))
    wrr = c(inp["L1_ffn_w_router"])
    T = 2 * S
    XF = np.empty((NCORE, T, D), dtype=f32)
    COMB = np.empty((NCORE, T, 8), dtype=f32)
    maps = [dict(h=c(hbuf[2 * ci:2 * ci + 2].reshape(T, D)), g2=g2r, wr=wrr, idn=idn)
            for ci in range(NCORE)]
    rr = _run(ncp, maps)
    for ci in range(NCORE):
        XF[ci] = rr[ci]["XF"]
        COMB[ci] = rr[ci]["COMB"]

    ncp = _build("moe", build_moe)
    acc = np.zeros((NCORE, T, D), dtype=f32)
    for e in range(8):
        wgx = c(inp["L1_ffn_wg"][e])
        wux = c(inp["L1_ffn_wu"][e])
        wdx = c(inp["L1_ffn_wd"][e])
        maps = [dict(xf=c(XF[ci]), wg=wgx, wu=wux, wd=wdx,
                     ccol=c(COMB[ci][:, e:e + 1]), accin=c(acc[ci]), idn=idn)
                for ci in range(NCORE)]
        rr = _run(ncp, maps)
        for ci in range(NCORE):
            acc[ci] = rr[ci]["accout"]

    ncp = _build("final", build_final)
    maps = [dict(x3=c(x3[2 * ci:2 * ci + 2].reshape(T, D)),
                 hh=c(hbuf[2 * ci:2 * ci + 2].reshape(T, D)),
                 acc=c(acc[ci]), idn=idn) for ci in range(NCORE)]
    rr = _run(ncp, maps)
    out = np.empty_like(x)
    for ci in range(NCORE):
        out[2 * ci:2 * ci + 2] = rr[ci]["x4"].reshape(2, S, D)
    return out
